# version 02 (final)
# baseline (speedup 1.0000x reference)
import os
import numpy as np
from contextlib import ExitStack

import concourse.bass as bass
import concourse.mybir as mybir
from concourse.bass_utils import run_bass_kernel_spmd

F32 = mybir.dt.float32
BF16 = mybir.dt.bfloat16
I32 = mybir.dt.int32
U8 = mybir.dt.uint8
AF = mybir.ActivationFunctionType
ALU = mybir.AluOpType
AX = mybir.AxisListType
RG = [list(range(8))]

B, L, D = 16, 2048, 1024
NB = 2
TOK = 4096
HTOK = 2048
NDT = 8
DI = 2048
NOCT = 16
DS = 16
DTR = 64
HF = 3072
BLF = 4096
MF = 4096
NE = 8
EPS_LN = 1e-5
EPS_RMS = 1e-6
MAGIC = float(3 << 22)
LN1P_C = [0.9999016422, -0.497875045, 0.3176489274, -0.1937606946, 0.0855697064, -0.01833878962]

IMPLEMENTED = "a3"
STAGES = os.environ.get("MK_STAGES", "full")
DEBUG = os.environ.get("MK_DEBUG", "")
SUBLAYERS = ["a1", "a2", "a3", "a4", "xc", "b1", "b2"]
if STAGES == "full" or SUBLAYERS.index(STAGES) > SUBLAYERS.index(IMPLEMENTED):
    STAGES = IMPLEMENTED


def stage_on(name):
    return STAGES == "full" or SUBLAYERS.index(name) <= SUBLAYERS.index(STAGES)


PHASE_B = stage_on("xc")


class K:
    ENGS = ("sync", "gpsimd", "scalar", "vector", "tensor")
    DCH = {"sync": "dSY", "scalar": "dSC", "gpsimd": "dGP"}

    def __init__(self, nc, st):
        self.nc = nc
        self.st = st
        self.ops = {e: [] for e in self.ENGS}
        self._sems = {}
        self._cnt = {}
        self.nbar = 0
        self.chan_owner = {}

    def sem(self, name):
        if name not in self._sems:
            self._sems[name] = self.st.enter_context(self.nc.semaphore(name))
            self._cnt[name] = 0
        return self._sems[name]

    def val(self, name):
        self.sem(name)
        return self._cnt[name]

    def emit(self, e, fn):
        self.ops[e].append(fn)

    def wait(self, e, name, v):
        if v <= 0:
            return
        s = self.sem(name)
        self.emit(e, lambda eng, s=s, v=v: eng.wait_ge(s, v))

    def do(self, e, fn, inc=None, n=1):
        if inc is None:
            self.emit(e, fn)
            return None
        s = self.sem(inc)
        self._cnt[inc] += n
        self.emit(e, lambda eng, fn=fn, s=s, n=n: fn(eng).then_inc(s, n))
        return self._cnt[inc]

    def dma(self, e, out, in_, ch=None):
        ch = ch or self.DCH[e]
        self.chan_owner[ch] = e
        s = self.sem(ch)
        self._cnt[ch] += 16
        self.emit(e, lambda eng, out=out, in_=in_, s=s: eng.dma_start(out=out, in_=in_).then_inc(s, 16))
        return (ch, self._cnt[ch])

    def drain(self, e):
        self.emit(e, lambda eng: eng.drain())

    def enop(self, e, inc, n=1):
        assert e in ("vector", "gpsimd")
        return self.do(e, lambda eng: eng.engine_nop(), inc=inc, n=n)

    def barrier(self):
        bar = self.sem("bar")
        for e in self.ENGS:
            for ch, oe in self.chan_owner.items():
                if oe == e and self.val(ch) > 0:
                    self.wait(e, ch, self.val(ch))
            self._cnt["bar"] += 1
            self.emit(e, lambda eng, s=bar: eng.drain().then_inc(s, 1))
        self.nbar += 1
        tgt = 5 * self.nbar
        assert self._cnt["bar"] == tgt
        for e in self.ENGS:
            self.wait(e, "bar", tgt)


class Arena:

    def __init__(self, X, free_half):
        self.X = X
        self.h = free_half
        self.used = [0] * NDT

    def alloc(self, units, cols_f32, dtype=F32):
        out = []
        for _ in range(units):
            for dt in range(NDT):
                if 2048 - self.used[dt] >= cols_f32:
                    b0 = dt * TOK + self.h * HTOK + self.used[dt]
                    self.used[dt] += cols_f32
                    v = self.X[:, b0:b0 + cols_f32]
                    if dtype != F32:
                        v = v.bitcast(dtype)
                    out.append(v)
                    break
            else:
                raise RuntimeError(f"arena OOM {units}x{cols_f32} used={self.used}")
        return out

    def alloc1(self, cols_f32, dtype=F32):
        return self.alloc(1, cols_f32, dtype)[0]


def build():
    nc = bass.Bass("TRN2", target_bir_lowering=False)
    d = {}

    def di(name, shape, dtype):
        d[name] = nc.dram_tensor(name, shape, dtype, kind="ExternalInput").ap()

    di("x", [NB, L, D], F32)
    di("ipw", [2 * DI, D], F32)
    di("cw", [DI, 4], F32)
    di("cb", [DI], F32)
    di("xpw", [DTR + 2 * DS, DI], F32)
    di("dpw", [DI, DTR], F32)
    di("dpb", [DI], F32)
    di("alog", [DI, DS], F32)
    di("dvec", [DI], F32)
    di("opw", [D, DI], F32)
    di("n1w", [D], F32)
    di("blg", [BLF, D], F32)
    di("blu", [BLF, D], F32)
    di("bld", [D, BLF], F32)
    di("n2w", [D], F32)
    di("fg", [HF, D], I32)
    di("fu", [HF, D], I32)
    di("fd", [D, HF], I32)
    di("wq", [D], F32)
    di("wk", [D], F32)
    di("wv", [D], F32)
    di("wo", [D], F32)
    di("gw", [D, NE], F32)
    di("w1", [NE, D, MF], F32)
    di("w2", [NE, MF, D], F32)
    di("w3", [NE, D, MF], F32)

    if PHASE_B:
        y = nc.dram_tensor("y", [B, L // 8, D], F32, kind="ExternalOutput").ap()
    else:
        y = nc.dram_tensor("y", [NB, L, D], F32, kind="ExternalOutput").ap()
    dbg = nc.dram_tensor("dbg", [128, 16384], F32, kind="ExternalOutput").ap() if DEBUG else None

    wt_ip = nc.dram_tensor("wt_ip", [D, 2 * DI], BF16, kind="Internal").ap()
    wt_op = nc.dram_tensor("wt_op", [DI, D], BF16, kind="Internal").ap()
    wt_xp = nc.dram_tensor("wt_xp", [DI, 128], F32, kind="Internal").ap()
    wt_dp = nc.dram_tensor("wt_dp", [DTR, DI], F32, kind="Internal").ap()
    wt_fg = nc.dram_tensor("wt_fg", [D, HF], BF16, kind="Internal").ap()
    wt_fu = nc.dram_tensor("wt_fu", [D, HF], BF16, kind="Internal").ap()
    wt_fd = nc.dram_tensor("wt_fd", [HF, D], BF16, kind="Internal").ap()
    wt_bg = nc.dram_tensor("wt_bg", [D, BLF], BF16, kind="Internal").ap()
    wt_bu = nc.dram_tensor("wt_bu", [D, BLF], BF16, kind="Internal").ap()
    wt_bd = nc.dram_tensor("wt_bd", [BLF, D], BF16, kind="Internal").ap()
    wb_1 = nc.dram_tensor("wb_1", [NE, D, MF], BF16, kind="Internal").ap()
    wb_3 = nc.dram_tensor("wb_3", [NE, D, MF], BF16, kind="Internal").ap()
    wb_2 = nc.dram_tensor("wb_2", [NE, MF, D], BF16, kind="Internal").ap()
    spl = nc.dram_tensor("spl", [2, NDT, 128, HTOK], F32, kind="Internal").ap()
    h1s = nc.dram_tensor("h1s", [TOK, D], F32, kind="Internal").ap()

    with ExitStack() as st:
        X = st.enter_context(nc.sbuf_tensor("X", [128, NDT * TOK], F32)).ap()
        XN = st.enter_context(nc.sbuf_tensor("XN", [128, NDT * 512], BF16)).ap()
        AUX = st.enter_context(nc.sbuf_tensor("AUX", [128, 2048], F32)).ap()
        IDN = st.enter_context(nc.sbuf_tensor("IDN", [128, 128], BF16)).ap()
        IDNF = st.enter_context(nc.sbuf_tensor("IDNF", [128, 128], F32)).ap()
        IOA = st.enter_context(nc.sbuf_tensor("IOA", [128, 128], I32)).ap()
        IOB = st.enter_context(nc.sbuf_tensor("IOB", [128, 128], I32)).ap()
        ONE = st.enter_context(nc.sbuf_tensor("ONE", [128, 128], F32)).ap()
        ROWS = st.enter_context(nc.sbuf_tensor("ROWS", [128, 4096], F32)).ap()
        WBUF = st.enter_context(nc.sbuf_tensor("WBUF", [128, 8192], BF16)).ap()
        MSK = st.enter_context(nc.sbuf_tensor("MSK", [128, 512], U8)).ap()
        OH16 = st.enter_context(nc.sbuf_tensor("OH16", [16, 2048], F32)).ap()
        pb = [st.enter_context(nc.psum_tensor(f"pb{i}", [128, 512], F32)).ap() for i in range(8)]

        k = K(nc, st)
        WSTGF = WBUF.bitcast(F32)

        def aux(c0, c1):
            return AUX[:, c0:c1]

        cur_set = [None]

        def actset(grp):
            if cur_set[0] != grp:
                cur_set[0] = grp
                f = AF.Exp if grp == 6 else AF.Silu
                k.emit("scalar", lambda e, f=f: e.preload_act_table(f))

        gv = k.do("gpsimd", lambda e: e.iota(IOA[:, :], pattern=[[1, 128]], base=0, channel_multiplier=0))
        gv = k.do("gpsimd", lambda e: e.iota(IOB[:, :], pattern=[[0, 128]], base=0, channel_multiplier=1), inc="g")
        k.wait("vector", "g", gv)
        k.do("vector", lambda e: e.tensor_tensor(IDN[:, :], IOA[:, :], IOB[:, :], ALU.is_equal))
        k.do("vector", lambda e: e.tensor_tensor(IDNF[:, :], IOA[:, :], IOB[:, :], ALU.is_equal))
        k.do("vector", lambda e: e.memset(ONE[:, :], 1.0))
        k.do("vector", lambda e: e.memset(AUX[:, :], 0.0))
        k.do("vector", lambda e: e.memset(ROWS[:, :], 0.0))
        k.do("vector", lambda e: e.memset(XN[:, :], 0.0))
        k.do("vector", lambda e: e.memset(MSK[:, :], 0))
        ioa16 = WSTGF[0:16, 0:2048].bitcast(I32)
        iob16 = WSTGF[0:16, 2048:4096].bitcast(I32)
        k.do("gpsimd", lambda e: e.iota(ioa16.rearrange("p (a b) -> p a b", a=16), pattern=[[1, 16], [0, 128]], base=0, channel_multiplier=0))
        gv2 = k.do("gpsimd", lambda e: e.iota(iob16.rearrange("p (a b) -> p a b", a=16), pattern=[[0, 16], [0, 128]], base=0, channel_multiplier=1), inc="g")
        k.wait("vector", "g", gv2)
        k.do("vector", lambda e: e.tensor_tensor(OH16[:, :], ioa16, iob16, ALU.is_equal))
        k.drain("vector")
        k.barrier()

        xflat = d["x"].rearrange("b l dd -> (b l) dd")
        tstgs = [WSTGF[:, 0:1024], WSTGF[:, 1024:2048]]
        ldt_hist = []
        for tg in range(TOK // 128):
            tstg = tstgs[tg % 2]
            if tg >= 2:
                k.wait("sync", "ldt", ldt_hist[tg - 2])
            ch, v = k.dma("sync", tstg, xflat[tg * 128:(tg + 1) * 128, :], ch="dLX%d" % (tg % 2))
            k.wait("tensor", ch, v)
            for dt in range(NDT):
                bank = pb[dt % 2]
                idx = tg * NDT + dt
                if idx >= 2:
                    k.wait("tensor", "ldx", idx - 1)
                tv = k.do("tensor", lambda e, b=bank, dt=dt, t_=tstg: e.transpose(b[:, 0:128], t_[:, dt * 128:(dt + 1) * 128], IDNF[:, :]),
                          inc=("ldt" if dt == NDT - 1 else "tmm"))
                k.wait("vector", ("ldt" if dt == NDT - 1 else "tmm"), tv)
                k.do("vector", lambda e, b=bank, dt=dt, tg=tg: e.tensor_copy(X[:, dt * TOK + tg * 128: dt * TOK + (tg + 1) * 128], b[:, 0:128]))
                k.enop("vector", "ldx")
            ldt_hist.append(k.val("ldt"))
        k.barrier()

        def load_vec_fm(src_flat, nrow, dst_cols, ncol_each=1):
            ch, v = k.dma("sync", WSTGF[0:nrow, 0:128 * ncol_each], src_flat)
            k.wait("tensor", ch, v)
            for j in range(ncol_each):
                bank = pb[j % 2]
                if ncol_each == 1:
                    src_v = WSTGF[0:nrow, 0:128]
                else:
                    src_v = WSTGF[0:nrow, 0:128 * ncol_each].rearrange("o (p j) -> o p j", j=ncol_each)[:, :, j]
                if k.val("pfr") > 0:
                    k.wait("tensor", "pfr", k.val("pfr"))
                tv = k.do("tensor", lambda e, b=bank, s=src_v, n=nrow: e.transpose(b[:, 0:n], s, IDNF[0:n, 0:n]), inc="tmm")
                k.wait("vector", "tmm", tv)
                if ncol_each == 1:
                    dst_v = dst_cols
                else:
                    dst_v = dst_cols.rearrange("p (o j) -> p o j", j=ncol_each)[:, :, j]
                k.do("vector", lambda e, b=bank, dv=dst_v, n=nrow: e.tensor_copy(dv, b[:, 0:n]), inc="pfr")
                k.drain("vector")
            vv = k.enop("vector", "v")
            k.wait("sync", "v", vv)
            k.wait("tensor", "v", vv)

        load_vec_fm(d["cw"].rearrange("(o p) j -> o (p j)", p=128), NOCT, aux(0, 64), ncol_each=4)
        load_vec_fm(d["cb"].rearrange("(o p) -> o p", p=128), NOCT, aux(64, 80))
        load_vec_fm(d["dpb"].rearrange("(o p) -> o p", p=128), NOCT, aux(80, 96))
        load_vec_fm(d["alog"].rearrange("(o p) n -> o (p n)", p=128), NOCT, aux(96, 352), ncol_each=16)
        load_vec_fm(d["dvec"].rearrange("(o p) -> o p", p=128), NOCT, aux(352, 368))
        for i, nm in enumerate(["wq", "wk", "wv"]):
            load_vec_fm(d[nm].rearrange("(t p) -> t p", p=128), NDT, ROWS[:, 3650 + 8 * i:3658 + 8 * i])
        load_vec_fm(d["wo"].rearrange("(t p) -> t p", p=128), NDT, aux(384, 392))
        load_vec_fm(d["n1w"].rearrange("(t p) -> t p", p=128), NDT, aux(392, 400))
        load_vec_fm(d["n2w"].rearrange("(t p) -> t p", p=128), NDT, aux(400, 408))
        k.barrier()

        actset(6)
        sv = k.do("scalar", lambda e: e.activation(out=aux(96, 352), in_=aux(96, 352), func=AF.Exp), inc="s")
        k.wait("vector", "s", sv)
        k.do("vector", lambda e: e.tensor_scalar_mul(aux(96, 352), aux(96, 352), -1.0))
        sv = k.do("scalar", lambda e: e.activation(out=ROWS[:, 3650:3674], in_=ROWS[:, 3650:3674], func=AF.Sign), inc="s")
        k.wait("vector", "s", sv)
        k.do("vector", lambda e: e.tensor_tensor(aux(368, 376), ROWS[:, 3650:3658], ROWS[:, 3658:3666], ALU.mult))
        k.do("vector", lambda e: e.tensor_copy(aux(376, 384), ROWS[:, 3666:3674]))
        vv = k.enop("vector", "v")
        k.wait("scalar", "v", vv)
        k.do("scalar", lambda e: e.activation(out=aux(384, 392), in_=aux(384, 392), func=AF.Sign))
        k.barrier()

        def prep_transpose(src_ap, O, I, dst, tern_rws_col=None, int_src=False, out_f32=False):
            noc = O // 128
            ICH = 2048
            for oc in range(noc):
                for i0_ in range(0, I, ICH):
                    ic_w = min(ICH, I - i0_)
                    rows = src_ap[oc * 128:(oc + 1) * 128, i0_:i0_ + ic_w]
                    for we in ("sync", "gpsimd"):
                        for sm in ("tmm", "v", "pfr"):
                            if k.val(sm) > 0:
                                k.wait(we, sm, k.val(sm))
                    if int_src:
                        ch, v = k.dma("sync", WSTGF[:, 0:ic_w].bitcast(I32), rows)
                        k.wait("vector", ch, v)
                        k.do("vector", lambda e, w=ic_w: e.tensor_copy(WBUF[:, 4096:4096 + w], WSTGF[:, 0:w].bitcast(I32)))
                        k.drain("vector")
                        vv = k.enop("vector", "v")
                        k.wait("tensor", "v", vv)
                        wsrc = lambda ic: WBUF[:, 4096 + ic * 128:4096 + (ic + 1) * 128]
                    elif tern_rws_col is not None:
                        ch, v = k.dma("sync", WSTGF[:, 0:ic_w], rows)
                        k.wait("vector", ch, v)
                        c = tern_rws_col
                        k.do("vector", lambda e, w=ic_w, c=c: e.tensor_scalar_mul(WSTGF[:, 0:w], WSTGF[:, 0:w], AUX[:, c:c + 1]))
                        k.drain("vector")
                        k.do("vector", lambda e, w=ic_w: e.tensor_scalar(WSTGF[:, 0:w], WSTGF[:, 0:w], MAGIC, -MAGIC, ALU.add, ALU.add))
                        k.drain("vector")
                        k.do("vector", lambda e, w=ic_w: e.tensor_scalar(WBUF[:, 4096:4096 + w], WSTGF[:, 0:w], -1.0, 1.0, ALU.max, ALU.min))
                        k.drain("vector")
                        vv = k.enop("vector", "v")
                        k.wait("tensor", "v", vv)
                        wsrc = lambda ic: WBUF[:, 4096 + ic * 128:4096 + (ic + 1) * 128]
                    elif out_f32:
                        ch, v = k.dma("sync", WSTGF[:, 0:ic_w], rows)
                        k.wait("tensor", ch, v)
                        wsrc = lambda ic: WSTGF[:, ic * 128:(ic + 1) * 128]
                    else:
                        ch, v = k.dma("gpsimd", WBUF[:, 4096:4096 + ic_w], rows)
                        k.wait("tensor", ch, v)
                        wsrc = lambda ic: WBUF[:, 4096 + ic * 128:4096 + (ic + 1) * 128]
                    for ic in range((ic_w + 127) // 128):
                        bw = min(128, ic_w - ic * 128)
                        bank = pb[(ic % 2)]
                        if k.val("pfr") > 0:
                            k.wait("tensor", "pfr", k.val("pfr"))
                        if out_f32:
                            tv = k.do("tensor", lambda e, b=bank, sv_=wsrc(ic)[:, 0:bw], bw=bw: e.transpose(b[0:bw, 0:128], sv_, IDNF[:, :]), inc="tmm")
                        else:
                            tv = k.do("tensor", lambda e, b=bank, sv_=wsrc(ic)[:, 0:bw], bw=bw: e.transpose(b.bitcast(BF16)[0:bw, 0:128], sv_, IDN[:, :]), inc="tmm")
                        k.wait("vector", "tmm", tv)
                        if k.val("dSC") > 0:
                            k.wait("vector", "dSC", k.val("dSC"))
                        if out_f32:
                            stg = WSTGF[:, 3968:4096]
                            k.do("vector", lambda e, b=bank, s2=stg[0:bw, :], bw=bw: e.tensor_copy(s2, b[0:bw, 0:128]), inc="pfr")
                            dsrc2 = stg[0:bw, :]
                        else:
                            stg = WBUF[:, 7680:7808]
                            k.do("vector", lambda e, b=bank, s2=stg[0:bw, :], bw=bw: e.tensor_copy(s2, b.bitcast(BF16)[0:bw, 0:128]), inc="pfr")
                            dsrc2 = stg[0:bw, :]
                        k.drain("vector")
                        vv = k.enop("vector", "v")
                        k.wait("scalar", "v", vv)
                        k.dma("scalar", dst[i0_ + ic * 128:i0_ + ic * 128 + bw, oc * 128:(oc + 1) * 128], dsrc2)
            k.barrier()

        if stage_on("a1"):
            prep_transpose(d["fg"], HF, D, wt_fg, int_src=True)
            prep_transpose(d["fu"], HF, D, wt_fu, int_src=True)
            prep_transpose(d["fd"], D, HF, wt_fd, int_src=True)

        if stage_on("a2"):
            prep_transpose(d["ipw"], 2 * DI, D, wt_ip)
            prep_transpose(d["opw"], D, DI, wt_op)
            prep_transpose(d["dpw"], DI, DTR, wt_dp, out_f32=True)
            k.do("vector", lambda e: e.memset(WSTGF[:, 0:128], 0.0))
            k.drain("vector")
            vv = k.enop("vector", "v")
            k.wait("scalar", "v", vv)
            for ic in range(DI // 128):
                k.dma("scalar", wt_xp[ic * 128:(ic + 1) * 128, :], WSTGF[:, 0:128])
            k.barrier()
            k.do("vector", lambda e: e.memset(WSTGF[:, 0:2048], 0.0))
            k.drain("vector")
            vv = k.enop("vector", "v")
            k.wait("sync", "v", vv)
            ch, v = k.dma("sync", WSTGF[0:96, 0:DI], d["xpw"][:, :])
            k.wait("tensor", ch, v)
            for ic in range(DI // 128):
                bank = pb[ic % 2]
                if k.val("pfr") > 0:
                    k.wait("tensor", "pfr", k.val("pfr"))
                tv = k.do("tensor", lambda e, b=bank, ic=ic: e.transpose(b[:, 0:96], WSTGF[0:96, ic * 128:(ic + 1) * 128], IDNF[0:96, 0:96]), inc="tmm")
                k.wait("vector", "tmm", tv)
                if k.val("dSC") > 0:
                    k.wait("vector", "dSC", k.val("dSC"))
                k.do("vector", lambda e, b=bank: e.tensor_copy(WSTGF[:, 3968:3968 + 96], b[:, 0:96]), inc="pfr")
                k.drain("vector")
                vv = k.enop("vector", "v")
                k.wait("scalar", "v", vv)
                k.dma("scalar", wt_xp[ic * 128:(ic + 1) * 128, 0:96], WSTGF[:, 3968:3968 + 96])
            k.barrier()
            for nm, dst_t, wcol in (("blg", wt_bg, 408), ("blu", wt_bu, 409), ("bld", wt_bd, 410)):
                src = d[nm]
                O, I = (BLF, D) if nm in ("blg", "blu") else (D, BLF)
                k.do("vector", lambda e: e.memset(ROWS[:, 3648:3649], 0.0))
                k.drain("vector")
                for oc in range(O // 128):
                    if oc > 0:
                        k.wait("sync", "v", k.val("v"))
                    ch, v = k.dma("sync", WSTGF[:, 0:I], src[oc * 128:(oc + 1) * 128, :])
                    k.wait("vector", ch, v)
                    k.do("vector", lambda e, I=I: e.reduce_sum(ROWS[:, 3649:3650], WSTGF[:, 0:I], AX.X, apply_absolute_value=True))
                    k.drain("vector")
                    k.do("vector", lambda e: e.tensor_add(ROWS[:, 3648:3649], ROWS[:, 3648:3649], ROWS[:, 3649:3650]))
                    k.drain("vector")
                    k.enop("vector", "v")
                vv = k.enop("vector", "v")
                k.wait("tensor", "v", vv)
                tv = k.do("tensor", lambda e: e.matmul(pb[2][0:1, 0:1], ONE[:, 0:1], ROWS[:, 3648:3649], start=True, stop=True), inc="tmm")
                k.wait("vector", "tmm", tv)
                k.do("vector", lambda e, OI=float(O * I): e.tensor_scalar(AUX[0:1, 420:421], pb[2][0:1, 0:1], 1.0 / OI, 1e-5, ALU.mult, ALU.max))
                k.drain("vector")
                k.do("vector", lambda e: e.reciprocal(AUX[0:1, 421:422], AUX[0:1, 420:421]))
                k.drain("vector")
                vv = k.enop("vector", "v")
                k.wait("tensor", "v", vv)
                tv = k.do("tensor", lambda e: e.matmul(pb[2][:, 0:1], ONE[0:1, 0:128], AUX[0:1, 421:422], start=True, stop=True), inc="tmm")
                k.wait("vector", "tmm", tv)
                k.do("vector", lambda e, c=wcol: e.tensor_copy(AUX[:, c:c + 1], pb[2][:, 0:1]))
                k.drain("vector")
                vv = k.enop("vector", "v")
                k.wait("tensor", "v", vv)
                tv = k.do("tensor", lambda e: e.matmul(pb[2][:, 0:1], ONE[0:1, 0:128], AUX[0:1, 420:421], start=True, stop=True), inc="tmm")
                k.wait("vector", "tmm", tv)
                k.do("vector", lambda e, c=wcol: e.tensor_copy(AUX[:, c + 16:c + 17], pb[2][:, 0:1]))
                k.drain("vector")
                k.barrier()
                prep_transpose(src, O, I, dst_t, tern_rws_col=wcol)

        if stage_on("a4"):
            for nm, dst_t in (("w1", wb_1), ("w3", wb_3), ("w2", wb_2)):
                src = d[nm]
                O, I = (D, MF) if nm != "w2" else (MF, D)
                for e_i in range(NE):
                    for og in range(O // 128):
                        ch, v = k.dma("gpsimd", WBUF[:, 0:I], src[e_i, og * 128:(og + 1) * 128, :])
                        k.wait("gpsimd", ch, v)
                        ch2, v2 = k.dma("gpsimd", dst_t[e_i, og * 128:(og + 1) * 128, :], WBUF[:, 0:I])
                        k.wait("gpsimd", ch2, v2)
        k.barrier()

        def spill(half):
            for dt in range(NDT):
                k.dma("sync", spl[half, dt, :, :], X[:, dt * TOK + half * HTOK: dt * TOK + half * HTOK + HTOK])
            k.barrier()

        def restore(half):
            for dt in range(NDT):
                k.dma("sync", X[:, dt * TOK + half * HTOK: dt * TOK + half * HTOK + HTOK], spl[half, dt, :, :])
            k.barrier()

        R_MEAN, R_VAR, R_TMP, R_INV, R_TMP2, R_BROW = 0, 512, 1024, 1536, 2048, 2560

        def row(c, w=512):
            return ROWS[0:1, c:c + w]

        def emit_rsqrt(out_c, in_c, eps, w):
            k.do("vector", lambda e: e.tensor_scalar_add(row(out_c, w), row(in_c, w), eps))
            k.drain("vector")
            k.do("vector", lambda e: e.tensor_scalar_mul(row(R_TMP, w), row(out_c, w), 0.5))
            k.drain("vector")
            oi = row(out_c, w).bitcast(I32)
            k.do("vector", lambda e: e.tensor_scalar(oi, oi, 1, None, ALU.logical_shift_right))
            k.drain("vector")
            k.do("vector", lambda e: e.tensor_scalar(oi, oi, -1, 0x5f3759df, ALU.mult, ALU.add))
            k.drain("vector")
            for _ in range(2):
                k.do("vector", lambda e: e.tensor_tensor(row(R_TMP2, w), row(out_c, w), row(out_c, w), ALU.mult))
                k.drain("vector")
                k.do("vector", lambda e: e.tensor_tensor(row(R_TMP2, w), row(R_TMP2, w), row(R_TMP, w), ALU.mult))
                k.drain("vector")
                k.do("vector", lambda e: e.tensor_scalar(row(R_TMP2, w), row(R_TMP2, w), -1.0, 1.5, ALU.mult, ALU.add))
                k.drain("vector")
                k.do("vector", lambda e: e.tensor_tensor(row(out_c, w), row(out_c, w), row(R_TMP2, w), ALU.mult))
                k.drain("vector")

        def emit_ln(tok0, sq, w=512):
            for dt in range(NDT):
                xs = X[:, dt * TOK + tok0: dt * TOK + tok0 + w]
                if dt > 0:
                    k.wait("scalar", "tmm", k.val("tmm"))
                sv = k.do("scalar", lambda e, xs=xs, s=sq, w=w: e.activation(out=s[:, 0:w], in_=xs, func=AF.Square), inc="s")
                k.wait("tensor", "s", sv)
                k.do("tensor", lambda e, xs=xs, dt=dt, w=w: e.matmul(pb[4][0:1, 0:w], ONE[:, 0:1], xs, start=(dt == 0), stop=(dt == NDT - 1)))
                tv = k.do("tensor", lambda e, s=sq, dt=dt, w=w: e.matmul(pb[5][0:1, 0:w], ONE[:, 0:1], s[:, 0:w], start=(dt == 0), stop=(dt == NDT - 1)), inc="tmm")
            k.wait("vector", "tmm", tv)
            k.do("vector", lambda e, w=w: e.tensor_scalar_mul(row(R_MEAN, w), pb[4][0:1, 0:w], 1.0 / D))
            k.do("vector", lambda e, w=w: e.tensor_scalar_mul(row(R_VAR, w), pb[5][0:1, 0:w], 1.0 / D))
            k.drain("vector")
            k.do("vector", lambda e, w=w: e.tensor_tensor(row(R_TMP, w), row(R_MEAN, w), row(R_MEAN, w), ALU.mult))
            k.drain("vector")
            k.do("vector", lambda e, w=w: e.tensor_tensor(row(R_VAR, w), row(R_VAR, w), row(R_TMP, w), ALU.subtract))
            k.drain("vector")
            emit_rsqrt(R_INV, R_VAR, EPS_LN, w)
            k.do("vector", lambda e, w=w: e.tensor_tensor(row(R_BROW, w), row(R_MEAN, w), row(R_INV, w), ALU.mult))
            k.drain("vector")
            k.do("vector", lambda e, w=w: e.tensor_scalar_mul(row(R_BROW, w), row(R_BROW, w), -1.0))
            k.drain("vector")
            vv = k.enop("vector", "v")
            k.wait("tensor", "v", vv)
            k.do("tensor", lambda e, w=w: e.matmul(pb[4][:, 0:w], ONE[0:1, 0:128], row(R_INV, w), start=True, stop=True))
            tv = k.do("tensor", lambda e, w=w: e.matmul(pb[5][:, 0:w], ONE[0:1, 0:128], row(R_BROW, w), start=True, stop=True), inc="tmm")
            k.wait("vector", "tmm", tv)
            for dt in range(NDT):
                xs = X[:, dt * TOK + tok0: dt * TOK + tok0 + w]
                k.do("vector", lambda e, xs=xs, s=sq, w=w: e.tensor_tensor(s[:, 0:w], xs, pb[4][:, 0:w], ALU.mult))
                k.drain("vector")
                k.do("vector", lambda e, s=sq, dt=dt, w=w: e.tensor_tensor(XN[:, dt * 512:dt * 512 + w], s[:, 0:w], pb[5][:, 0:w], ALU.add))
                k.drain("vector")
            return k.enop("vector", "v")

        def emit_routing(prev_xn_v):
            gwb = WBUF[:, 6144:6208]
            ch, v = k.dma("gpsimd", gwb.rearrange("p (kc e) -> p kc e", kc=8),
                          d["gw"].rearrange("(kc p) e -> p kc e", p=128))
            k.wait("tensor", ch, v)
            k.wait("tensor", "v", prev_xn_v)
            for kc in range(8):
                tv = k.do("tensor", lambda e, kc=kc: e.matmul(
                    pb[6][0:8, 0:512], gwb[:, kc * 8:(kc + 1) * 8], XN[:, kc * 512:(kc + 1) * 512],
                    start=(kc == 0), stop=(kc == 7)), inc=("tmm" if kc == 7 else None))
            k.wait("vector", "tmm", tv)
            lsb = ROWS[0:8, 3072:3584]
            k.do("vector", lambda e: e.tensor_copy(lsb, pb[6][0:8, 0:512]))
            k.drain("vector")
            vv = k.enop("vector", "v")
            k.wait("tensor", "v", vv)
            actset(6)
            lt = ROWS[:, 3584:3592]
            mx = ROWS[:, 3592:3600]
            tmp = ROWS[:, 3600:3608]
            tmp2 = ROWS[:, 3608:3616]
            ml = ROWS[:, 3616:3624]
            neg = ROWS[:, 3624:3632]
            oh1 = ROWS[:, 3632:3640]
            scat = ROWS[:, 3640:3648]
            for tg in range(4):
                if k.val("pfr") > 0 and tg > 0:
                    k.wait("tensor", "pfr", k.val("pfr"))
                tv = k.do("tensor", lambda e, tg=tg: e.transpose(pb[7][:, 0:8], lsb[:, tg * 128:(tg + 1) * 128], IDNF[:, :]), inc="tmm")
                k.wait("vector", "tmm", tv)
                k.do("vector", lambda e: e.tensor_copy(lt, pb[7][:, 0:8]), inc="pfr")
                k.do("vector", lambda e: e.memset(neg, -1e30))
                k.drain("vector")
                k.do("vector", lambda e: e.max(mx, lt))
                k.drain("vector")

                def slot(thr_col, base, mult_col):
                    thr = mx[:, thr_col:thr_col + 1]
                    k.do("vector", lambda e: e.tensor_scalar_mul(tmp, lt, -1.0))
                    k.drain("vector")
                    k.do("vector", lambda e: e.tensor_tensor(tmp, tmp, lt, ALU.max))
                    k.drain("vector")
                    k.do("vector", lambda e, thr=thr: e.tensor_scalar(tmp, tmp, thr, -0.02, ALU.max, ALU.mult))
                    k.drain("vector")
                    k.do("vector", lambda e, thr=thr: e.tensor_scalar(tmp2, lt, thr, None, ALU.subtract))
                    k.drain("vector")
                    k.do("vector", lambda e: e.tensor_tensor(MSK[:, 0:8], tmp2, tmp, ALU.is_ge))
                    k.drain("vector")
                    k.do("vector", lambda e, base=base: e.select(ml, MSK[:, 0:8], base, neg, add_drain=True))
                    k.drain("vector")
                    k.do("vector", lambda e, thr=thr: e.tensor_scalar(ml, ml, thr, None, ALU.subtract))
                    k.drain("vector")
                    sv = k.do("scalar", lambda e: e.activation(out=ml, in_=ml, func=AF.Exp), inc="s")
                    k.wait("vector", "s", sv)
                    k.do("vector", lambda e: e.reduce_sum(tmp[:, 0:1], ml, AX.X))
                    k.drain("vector")
                    k.do("vector", lambda e, mc=mult_col: e.reciprocal(mx[:, mc:mc + 1], tmp[:, 0:1]))
                    k.drain("vector")

                slot(0, lt, 6)
                k.do("vector", lambda e: e.tensor_scalar(oh1, lt, mx[:, 0:1], None, ALU.is_equal))
                k.do("vector", lambda e: e.tensor_scalar(MSK[:, 0:8], lt, mx[:, 0:1], None, ALU.is_equal))
                k.drain("vector")
                k.do("vector", lambda e: e.select(scat, MSK[:, 0:8], neg, lt, add_drain=True))
                k.drain("vector")
                slot(1, scat, 7)
                k.do("vector", lambda e: e.tensor_scalar(tmp, scat, mx[:, 1:2], None, ALU.is_equal))
                k.drain("vector")
                k.do("vector", lambda e: e.tensor_scalar_mul(oh1, oh1, mx[:, 6:7]))
                k.drain("vector")
                wtk = AUX[:, 512 + tg * 8:512 + tg * 8 + 8]
                k.do("vector", lambda e: e.tensor_scalar_mul(tmp, tmp, mx[:, 7:8]))
                k.drain("vector")
                k.do("vector", lambda e, wtk=wtk: e.tensor_tensor(wtk, tmp, oh1, ALU.add))
                k.drain("vector")
            return k.enop("vector", "v")

        def emit_ffn_like(kind, half):
            ar = Arena(X, 1 - half)
            sq = ar.alloc1(512)
            acc = ar.alloc(4, 1024)
            hall_u = ar.alloc(8, 1024)
            hall = []
            for u in hall_u:
                ub = u.bitcast(BF16)
                hall += [ub[:, i * 512:(i + 1) * 512] for i in range(4)]
            nfc = (HF if kind == "l0f" else MF) // 128
            nexp = 1 if kind == "l0f" else NE
            out_scale = 0.5 / (127.0 * 127.0) if kind == "l0f" else 0.5
            sscale = (1.0 / 127.0) if kind == "l0f" else 1.0
            wgb = [WBUF[:, 0:1024], WBUF[:, 1024:2048]]
            wub = [WBUF[:, 2048:3072], WBUF[:, 3072:4096]]
            wdb = [WBUF[:, 4096:5120], WBUF[:, 5120:6144]]
            actset(18)

            for tt in range(4):
                tok0 = half * HTOK + tt * 512
                xn_v = emit_ln(tok0, sq)
                k.wait("tensor", "v", xn_v)
                k.wait("scalar", "v", xn_v)
                if kind == "moe":
                    rt_v = emit_routing(xn_v)
                    k.wait("tensor", "v", rt_v)
                    actset(18)
                for a_ in acc:
                    k.do("vector", lambda e, a_=a_: e.memset(a_, 0.0))
                k.drain("vector")
                av = k.enop("vector", "v")
                k.wait("tensor", "v", av)
                gu_hist = []
                w2_hist = []
                for e_i in range(nexp):
                    for fc in range(nfc):
                        it = e_i * nfc + fc
                        buf = it % 2
                        if kind == "l0f":
                            gsrc = wt_fg.rearrange("(kc p) o -> p kc o", p=128)[:, :, fc * 128:(fc + 1) * 128]
                            usrc = wt_fu.rearrange("(kc p) o -> p kc o", p=128)[:, :, fc * 128:(fc + 1) * 128]
                        else:
                            gsrc = wb_1[e_i].rearrange("(kc p) o -> p kc o", p=128)[:, :, fc * 128:(fc + 1) * 128]
                            usrc = wb_3[e_i].rearrange("(kc p) o -> p kc o", p=128)[:, :, fc * 128:(fc + 1) * 128]
                        if it >= 2:
                            k.wait("sync", "tgu", gu_hist[it - 2][0])
                        chg, vg = k.dma("sync", wgb[buf].rearrange("p (kc o) -> p kc o", kc=8), gsrc, ch="dWG%d" % buf)
                        chu, vu = k.dma("sync", wub[buf].rearrange("p (kc o) -> p kc o", kc=8), usrc, ch="dWU%d" % buf)
                        k.wait("tensor", chg, vg)
                        k.wait("tensor", chu, vu)
                        if it >= 2:
                            k.wait("tensor", "s", gu_hist[it - 2][1])
                            k.wait("tensor", "hev", gu_hist[it - 2][2])
                        bg, bu = pb[0 + 2 * buf], pb[1 + 2 * buf]
                        for kc in range(8):
                            k.do("tensor", lambda e, b=bg, w_=wgb[buf], kc=kc: e.matmul(
                                b[:, 0:512], w_[:, kc * 128:(kc + 1) * 128], XN[:, kc * 512:(kc + 1) * 512],
                                start=(kc == 0), stop=(kc == 7)))
                        tgu_v = None
                        for kc in range(8):
                            tgu_v = k.do("tensor", lambda e, b=bu, w_=wub[buf], kc=kc: e.matmul(
                                b[:, 0:512], w_[:, kc * 128:(kc + 1) * 128], XN[:, kc * 512:(kc + 1) * 512],
                                start=(kc == 0), stop=(kc == 7)), inc=("tgu" if kc == 7 else None))
                        k.wait("scalar", "tgu", tgu_v)
                        if it >= 1:
                            k.wait("scalar", "hev", gu_hist[it - 1][2])
                        s_v = k.do("scalar", lambda e, b=bg, s=sq: e.activation(out=s[:, 0:512], in_=b[:, 0:512], func=AF.Silu, scale=sscale), inc="s")
                        k.wait("vector", "s", s_v)
                        hev_v = k.do("vector", lambda e, b=bu, s=sq, h=hall[fc]: e.tensor_tensor(h, s[:, 0:512], b[:, 0:512], ALU.mult), inc="hev")
                        gu_hist.append((tgu_v, s_v, hev_v))
                    k.wait("tensor", "hev", gu_hist[-1][2])
                    for th in range(2):
                        for fc in range(nfc):
                            it2 = (e_i * 2 + th) * nfc + fc
                            buf = it2 % 2
                            dsrc = wt_fd[fc * 128:(fc + 1) * 128, :] if kind == "l0f" else wb_2[e_i, fc * 128:(fc + 1) * 128, :]
                            if it2 >= 2:
                                k.wait("sync", "tw2", w2_hist[it2 - 2])
                            chd, vd = k.dma("sync", wdb[buf], dsrc, ch="dWD%d" % buf)
                            k.wait("tensor", chd, vd)
                            tw2_v = None
                            for tb in range(2):
                                for nh in range(2):
                                    bank = pb[4 + 2 * tb + nh]
                                    tw2_v = k.do("tensor", lambda e, bank=bank, h=hall[fc], th=th, tb=tb, nh=nh, buf=buf, fc=fc, nfc=nfc: e.matmul(
                                        bank[:, 0:512], h[:, th * 256 + tb * 128: th * 256 + (tb + 1) * 128],
                                        wdb[buf][:, nh * 512:(nh + 1) * 512],
                                        start=(fc == 0), stop=(fc == nfc - 1)),
                                        inc=("tw2" if (tb == 1 and nh == 1) else None))
                            w2_hist.append(tw2_v)
                        k.wait("vector", "tw2", k.val("tw2"))
                        for tb in range(2):
                            tbg = th * 2 + tb
                            for nh in range(2):
                                bank = pb[4 + 2 * tb + nh]
                                a_ = acc[tbg][:, nh * 512:(nh + 1) * 512]
                                if kind == "moe":
                                    wc = AUX[:, 512 + tbg * 8 + e_i: 512 + tbg * 8 + e_i + 1]
                                    k.do("vector", lambda e, b=bank, wc=wc: e.tensor_scalar_mul(sq[:, 0:512], b[:, 0:512], wc))
                                    k.drain("vector")
                                    k.do("vector", lambda e, a_=a_: e.tensor_tensor(a_, a_, sq[:, 0:512], ALU.add))
                                else:
                                    k.do("vector", lambda e, b=bank, a_=a_: e.tensor_tensor(a_, b[:, 0:512], a_, ALU.add))
                                k.drain("vector")
                        pv = k.enop("vector", "v")
                        k.wait("tensor", "v", pv)
                tr_hist = []
                for tbg in range(4):
                    for dt in range(NDT):
                        idx = tbg * NDT + dt
                        bank = pb[idx % 2]
                        if idx >= 2:
                            k.wait("tensor", "trf", tr_hist[idx - 2])
                        tv = k.do("tensor", lambda e, bank=bank, a_=acc[tbg], dt=dt: e.transpose(
                            bank[:, 0:128], a_[:, dt * 128:(dt + 1) * 128], IDNF[:, :]), inc="tmm")
                        k.wait("vector", "tmm", tv)
                        xs = X[:, dt * TOK + tok0 + tbg * 128: dt * TOK + tok0 + (tbg + 1) * 128]
                        k.do("vector", lambda e, bank=bank, xs=xs, sc=out_scale: e.scalar_tensor_tensor(xs, bank[:, 0:128], sc, xs, ALU.mult, ALU.add))
                        trf_v = k.enop("vector", "trf")
                        tr_hist.append(trf_v)
                k.barrier()

        def emit_ffn_both_halves(kind):
            spill(1)
            emit_ffn_like(kind, 0)
            spill(0)
            restore(1)
            emit_ffn_like(kind, 1)
            restore(0)

        XNF = XN.bitcast(F32)
        XNFREE = [XNF[:, dt * 256 + 128:(dt + 1) * 256] for dt in range(NDT)]

        def emit_stat_bcast_fm(src_pm, dst_fm, statc):
            for tg in range(4):
                if tg >= 1:
                    k.wait("tensor", "sbf", k.val("sbf"))
                tv = k.do("tensor", lambda e, tg=tg: e.transpose(pb[6][:, 0:128], src_pm[:, tg * 128:(tg + 1) * 128], IDNF[:, :]), inc="tmm")
                k.wait("vector", "tmm", tv)
                k.do("vector", lambda e, tg=tg: e.reduce_max(AUX[:, statc + tg:statc + tg + 1], pb[6][:, 0:128], AX.X))
                k.drain("vector")
                k.do("vector", lambda e, tg=tg: e.tensor_scalar_mul(ROWS[:, 3072:3200], ONE[:, 0:128], AUX[:, statc + tg:statc + tg + 1]))
                k.drain("vector")
                vv = k.enop("vector", "sbf")
                k.wait("tensor", "sbf", vv)
                tv = k.do("tensor", lambda e: e.transpose(pb[7][:, 0:128], ROWS[:, 3072:3200], IDNF[:, :]), inc="tmm")
                k.wait("vector", "tmm", tv)
                k.do("vector", lambda e, tg=tg: e.tensor_copy(dst_fm[:, tg * 128:(tg + 1) * 128], pb[7][:, 0:128]), inc="sbf")
            k.drain("vector")
            return k.enop("vector", "v")

        def emit_mamba_half(bl):
            ar = Arena(X, 1 - bl)
            TC = 256
            u_all = ar.alloc(NOCT, TC)
            sres_b = ar.alloc1(NOCT * TC // 2)
            SRB = sres_b.bitcast(BF16)
            uext = ar.alloc1(TC + 4)
            dlt = ar.alloc(2, TC)
            dlu = ar.alloc(2, TC)
            bcb = ar.alloc1(DS * TC // 2)
            ccb = ar.alloc1(DS * TC // 2)
            BBF, CBF = bcb.bitcast(BF16), ccb.bitcast(BF16)
            dag = ar.alloc1(4 * TC)
            dbg2 = ar.alloc1(4 * TC // 2)
            hg = ar.alloc1(4 * TC // 2)
            DBG, HGB = dbg2.bitcast(BF16), hg.bitcast(BF16)
            PRB = dag[:, 0:512].bitcast(BF16)
            yoct = ar.alloc1(TC)
            ybf = ar.alloc1(NOCT * TC // 2)
            YBF = ybf.bitcast(BF16)
            mcp = [ybf[:, dt * TC:(dt + 1) * TC] for dt in range(NDT)]
            sqln = ar.alloc1(TC)
            xdsa, xdsb = XNFREE[0], XNFREE[1]
            bBa, bBb = XNFREE[2], XNFREE[3]
            bCa, bCb = XNFREE[6], XNFREE[7]
            spr1, spr2 = ROWS[:, 3584:3712], ROWS[:, 3712:3840]
            spr = ROWS[:, 3584:3840]
            qbuf = ROWS[:, 3840:4096]
            wipb = [WBUF[:, 0:1024], WBUF[:, 1024:2048]]
            wrsb = [WBUF[:, 2048:3072], WBUF[:, 3072:4096]]
            wopb = [WBUF[:, 4096:5120], WBUF[:, 5120:6144]]
            XPST = WSTGF[:, 3072:3200]
            DPST = WSTGF[0:64, 3200:3328]
            H1ST = WSTGF[:, 0:1024]

            k.do("vector", lambda e: e.memset(AUX[:, 720:976], 0.0))
            k.do("vector", lambda e: e.memset(AUX[:, 976:1040], 0.0))
            k.drain("vector")
            zz = k.enop("vector", "v")
            k.wait("scalar", "v", zz)
            k.wait("tensor", "v", zz)

            for cp in range(8):
                tok0 = bl * HTOK + cp * TC
                actset(18)
                xn_v = emit_ln(tok0, sqln, TC)
                k.wait("tensor", "v", xn_v)
                k.wait("scalar", "v", xn_v)
                ip_mm, ip_ev = [], []
                for oct in range(NOCT):
                    buf = oct % 2
                    wsrc = wt_ip.rearrange("(kc p) o -> p kc o", p=128)[:, :, oct * 128:(oct + 1) * 128]
                    if oct == 0 and k.val("dGP") > 0:
                        k.wait("sync", "dGP", k.val("dGP"))
                    if oct >= 2:
                        k.wait("sync", "tip", ip_mm[oct - 2])
                    chw, vw = k.dma("sync", wipb[buf].rearrange("p (kc o) -> p kc o", kc=8), wsrc, ch="dIP%d" % buf)
                    k.wait("tensor", chw, vw)
                    if oct >= 2:
                        k.wait("tensor", "uev", ip_ev[oct - 2])
                    bk = pb[buf]
                    v_ = None
                    for kc in range(8):
                        v_ = k.do("tensor", lambda e, b=bk, w_=wipb[buf], kc=kc, TC=TC: e.matmul(
                            b[:, 0:TC], w_[:, kc * 128:(kc + 1) * 128], XN[:, kc * 512:kc * 512 + TC],
                            start=(kc == 0), stop=(kc == 7)), inc=("tip" if kc == 7 else None))
                    ip_mm.append(v_)
                    k.wait("vector", "tip", v_)
                    k.do("vector", lambda e, oct=oct: e.tensor_copy(uext[:, 1:4], AUX[:, 976 + oct * 4:979 + oct * 4]))
                    k.drain("vector")
                    ev_ = k.do("vector", lambda e, b=bk, TC=TC: e.tensor_copy(uext[:, 4:4 + TC], b[:, 0:TC]), inc="uev")
                    ip_ev.append(ev_)
                    k.drain("vector")
                    k.do("vector", lambda e, oct=oct, TC=TC: e.tensor_copy(AUX[:, 976 + oct * 4:979 + oct * 4], uext[:, 1 + TC:4 + TC]))
                    k.drain("vector")
                    uu = u_all[oct]
                    k.do("vector", lambda e, uu=uu, oct=oct, TC=TC: e.tensor_scalar(
                        uu, uext[:, 1:1 + TC], AUX[:, oct * 4:oct * 4 + 1], AUX[:, 64 + oct:65 + oct], ALU.mult, ALU.add))
                    k.drain("vector")
                    for j in range(1, 4):
                        k.do("vector", lambda e, oct=oct, j=j, TC=TC: e.tensor_scalar_mul(
                            sqln[:, 0:TC], uext[:, 1 + j:1 + j + TC], AUX[:, oct * 4 + j:oct * 4 + j + 1]))
                        k.drain("vector")
                        k.do("vector", lambda e, uu=uu, TC=TC: e.tensor_tensor(uu, uu, sqln[:, 0:TC], ALU.add))
                        k.drain("vector")
                    cv = k.enop("vector", "cvs")
                    k.wait("scalar", "cvs", cv)
                    k.do("scalar", lambda e, uu=uu: e.activation(out=uu, in_=uu, func=AF.Silu), inc="s")
                rs_mm, rs_sl = [], []
                for oct in range(NOCT):
                    buf = oct % 2
                    wsrc = wt_ip.rearrange("(kc p) o -> p kc o", p=128)[:, :, DI + oct * 128:DI + (oct + 1) * 128]
                    if oct >= 2:
                        k.wait("sync", "trs", rs_mm[oct - 2])
                    chw, vw = k.dma("sync", wrsb[buf].rearrange("p (kc o) -> p kc o", kc=8), wsrc, ch="dRS%d" % buf)
                    k.wait("tensor", chw, vw)
                    if oct >= 2:
                        k.wait("tensor", "s", rs_sl[oct - 2])
                    bk = pb[2 + buf]
                    v_ = None
                    for kc in range(8):
                        v_ = k.do("tensor", lambda e, b=bk, w_=wrsb[buf], kc=kc, TC=TC: e.matmul(
                            b[:, 0:TC], w_[:, kc * 128:(kc + 1) * 128], XN[:, kc * 512:kc * 512 + TC],
                            start=(kc == 0), stop=(kc == 7)), inc=("trs" if kc == 7 else None))
                    rs_mm.append(v_)
                    k.wait("scalar", "trs", v_)
                    sl = k.do("scalar", lambda e, b=bk, oct=oct, TC=TC: e.activation(
                        out=SRB[:, oct * TC:(oct + 1) * TC], in_=b[:, 0:TC], func=AF.Silu), inc="s")
                    rs_sl.append(sl)
                k.wait("tensor", "s", k.val("s"))
                for oct in range(NOCT):
                    if oct >= 1:
                        k.wait("sync", "tmm", k.val("tmm"))
                    chx, vx = k.dma("sync", XPST, wt_xp[oct * 128:(oct + 1) * 128, :], ch="dXP%d" % (oct % 2))
                    k.wait("tensor", chx, vx)
                    k.do("tensor", lambda e, oct=oct, TC=TC: e.matmul(pb[4][0:64, 0:TC], XPST[:, 0:64], u_all[oct],
                                                                     start=(oct == 0), stop=(oct == NOCT - 1)))
                    k.do("tensor", lambda e, oct=oct, TC=TC: e.matmul(pb[5][0:16, 0:TC], XPST[:, 64:80], u_all[oct],
                                                                     start=(oct == 0), stop=(oct == NOCT - 1)))
                    tv = k.do("tensor", lambda e, oct=oct, TC=TC: e.matmul(pb[7][0:16, 0:TC], XPST[:, 80:96], u_all[oct],
                                                                          start=(oct == 0), stop=(oct == NOCT - 1)), inc="tmm")
                k.wait("vector", "tmm", k.val("tmm"))
                k.do("vector", lambda e: e.tensor_copy(xdsa[0:64, :], pb[4][0:64, 0:128]))
                k.do("vector", lambda e: e.tensor_copy(xdsb[0:64, :], pb[4][0:64, 128:256]))
                k.do("vector", lambda e: e.tensor_copy(bBa[0:16, :], pb[5][0:16, 0:128]))
                k.do("vector", lambda e: e.tensor_copy(bBb[0:16, :], pb[5][0:16, 128:256]))
                k.do("vector", lambda e: e.tensor_copy(bCa[0:16, :], pb[7][0:16, 0:128]))
                k.do("vector", lambda e: e.tensor_copy(bCb[0:16, :], pb[7][0:16, 128:256]))
                k.drain("vector")
                vv = k.enop("vector", "v")
                k.wait("tensor", "v", vv)
                bc_ev = []
                for n in range(2 * DS):
                    bank = pb[4 + (n % 2)]
                    sa = bBa if n < DS else bCa
                    sb = bBb if n < DS else bCb
                    nn = n % DS
                    if n >= 2:
                        k.wait("tensor", "bcf", bc_ev[n - 2])
                    k.do("tensor", lambda e, nn=nn, bank=bank, sa=sa: e.matmul(
                        bank[:, 0:128], OH16[0:16, nn * 128:(nn + 1) * 128], sa[0:16, :], start=True, stop=True))
                    tv = k.do("tensor", lambda e, nn=nn, bank=bank, sb=sb: e.matmul(
                        bank[:, 128:256], OH16[0:16, nn * 128:(nn + 1) * 128], sb[0:16, :], start=True, stop=True), inc="tmm")
                    k.wait("vector", "tmm", tv)
                    dst = BBF[:, n * TC:(n + 1) * TC] if n < DS else CBF[:, (n - DS) * TC:(n - DS + 1) * TC]
                    ev_ = k.do("vector", lambda e, bank=bank, dst=dst, TC=TC: e.tensor_copy(dst, bank[:, 0:TC]), inc="bcf")
                    bc_ev.append(ev_)
                actset(6)
                dl_hist = []
                for oct in range(NOCT):
                    dbuf_i = oct % 2
                    if oct >= 1:
                        k.wait("sync", "dl", dl_hist[oct - 1][0])
                    chd, vd = k.dma("sync", DPST, wt_dp[:, oct * 128:(oct + 1) * 128], ch="dDP%d" % (oct % 2))
                    k.wait("tensor", chd, vd)
                    if oct >= 1:
                        k.wait("tensor", "dl", dl_hist[oct - 1][0])
                    k.do("tensor", lambda e: e.matmul(pb[6][:, 0:128], DPST, xdsa[0:64, :], start=True, stop=True))
                    tv = k.do("tensor", lambda e: e.matmul(pb[6][:, 128:256], DPST, xdsb[0:64, :], start=True, stop=True), inc="tmm")
                    k.wait("scalar", "tmm", tv)
                    dl, du = dlt[dbuf_i], dlu[dbuf_i]
                    k.do("scalar", lambda e, oct=oct: e.activation(out=spr, in_=pb[6][:, 0:256], func=AF.Abs, bias=AUX[:, 80 + oct:81 + oct]))
                    k.drain("scalar")
                    sv = k.do("scalar", lambda e: e.activation(out=spr, in_=spr, func=AF.Exp, scale=-1.0), inc="s")
                    k.wait("vector", "tmm", tv)
                    k.wait("vector", "s", sv)
                    if oct >= 2:
                        k.wait("vector", "s", dl_hist[oct - 2][1])
                        k.wait("vector", "g", dl_hist[oct - 2][2])
                    k.do("vector", lambda e, dl=dl, oct=oct: e.tensor_scalar(dl[:, 0:128], pb[6][:, 0:128], AUX[:, 80 + oct:81 + oct], None, ALU.add))
                    k.do("vector", lambda e, dl=dl, oct=oct: e.tensor_scalar(dl[:, 128:256], pb[6][:, 128:256], AUX[:, 80 + oct:81 + oct], None, ALU.add))
                    k.drain("vector")
                    vv = k.enop("vector", "v")
                    k.wait("tensor", "v", vv)
                    k.wait("gpsimd", "s", sv)
                    k.do("gpsimd", lambda e: e.tensor_scalar_mul(qbuf, spr, LN1P_C[5]))
                    k.drain("gpsimd")
                    for c_ in LN1P_C[4::-1]:
                        k.do("gpsimd", lambda e, c_=c_: e.tensor_scalar(qbuf, qbuf, c_, None, ALU.add))
                        k.drain("gpsimd")
                        k.do("gpsimd", lambda e: e.tensor_tensor(qbuf, qbuf, spr, ALU.mult))
                        k.drain("gpsimd")
                    spv = k.enop("gpsimd", "sp")
                    k.wait("vector", "sp", spv)
                    k.do("vector", lambda e, dl=dl: e.scalar_tensor_tensor(dl[:, 0:256], dl[:, 0:256], 0.0, qbuf, ALU.max, ALU.add))
                    k.drain("vector")
                    k.do("vector", lambda e, dl=dl, du=du, oct=oct: e.tensor_tensor(du, dl, u_all[oct], ALU.mult))
                    k.drain("vector")
                    dlv = k.enop("vector", "dl")
                    k.wait("scalar", "dl", dlv)
                    k.wait("gpsimd", "dl", dlv)
                    yy = yoct
                    s_end = g_end = None
                    for ng in range(4):
                        sv = gv = None
                        if k.val("sdn") > 0:
                            k.wait("scalar", "sdn", k.val("sdn"))
                            k.wait("gpsimd", "sdn", k.val("sdn"))
                        for j in range(4):
                            n = ng * 4 + j
                            sv = k.do("scalar", lambda e, dl=dl, j=j, oct=oct, n=n, TC=TC: e.activation(
                                out=dag[:, j * TC:(j + 1) * TC], in_=dl, func=AF.Exp,
                                scale=AUX[:, 96 + oct * 16 + n:97 + oct * 16 + n]), inc=("s" if j == 3 else None))
                            gv = k.do("gpsimd", lambda e, du=du, j=j, n=n, TC=TC: e.tensor_tensor(
                                DBG[:, j * TC:(j + 1) * TC], du, BBF[:, n * TC:(n + 1) * TC], ALU.mult), inc=("g" if j == 3 else None))
                        k.wait("vector", "s", sv)
                        k.wait("vector", "g", gv)
                        s_end, g_end = sv, gv
                        for j in range(4):
                            n = ng * 4 + j
                            stc = 720 + oct * 16 + n
                            k.do("vector", lambda e, j=j, stc=stc, TC=TC: e.tensor_tensor_scan(
                                HGB[:, j * TC:(j + 1) * TC], dag[:, j * TC:(j + 1) * TC], DBG[:, j * TC:(j + 1) * TC],
                                AUX[:, stc:stc + 1], ALU.mult, ALU.add))
                            k.drain("vector")
                        k.do("vector", lambda e, oct=oct, ng=ng, TC=TC: e.tensor_copy(
                            AUX[:, 720 + oct * 16 + ng * 4:720 + oct * 16 + ng * 4 + 4],
                            HGB.rearrange("p (j t) -> p j t", j=4)[:, :, TC - 1]))
                        k.drain("vector")
                        k.do("vector", lambda e, ng=ng, TC=TC: e.tensor_tensor(PRB[:, 0:4 * TC], HGB[:, 0:4 * TC],
                                                                               CBF[:, ng * 4 * TC:(ng + 1) * 4 * TC], ALU.mult))
                        k.drain("vector")
                        k.do("vector", lambda e, TC=TC: e.tensor_tensor(PRB[:, 0:2 * TC], PRB[:, 0:2 * TC], PRB[:, 2 * TC:4 * TC], ALU.add))
                        k.drain("vector")
                        if ng == 0:
                            k.do("vector", lambda e, yy=yy, TC=TC: e.tensor_tensor(yy, PRB[:, 0:TC], PRB[:, TC:2 * TC], ALU.add))
                        else:
                            k.do("vector", lambda e, TC=TC: e.tensor_tensor(PRB[:, 0:TC], PRB[:, 0:TC], PRB[:, TC:2 * TC], ALU.add))
                            k.drain("vector")
                            k.do("vector", lambda e, yy=yy, TC=TC: e.tensor_tensor(yy, yy, PRB[:, 0:TC], ALU.add))
                        k.drain("vector")
                        k.enop("vector", "sdn")
                    scnv = k.enop("vector", "scn")
                    k.do("vector", lambda e, oct=oct, TC=TC: e.tensor_scalar_mul(sqln[:, 0:TC], u_all[oct], AUX[:, 352 + oct:353 + oct]))
                    k.drain("vector")
                    k.do("vector", lambda e, yy=yy, TC=TC: e.tensor_tensor(yy, yy, sqln[:, 0:TC], ALU.add))
                    k.drain("vector")
                    oy = k.do("vector", lambda e, yy=yy, oct=oct, TC=TC: e.tensor_tensor(
                        YBF[:, oct * TC:(oct + 1) * TC], yy, SRB[:, oct * TC:(oct + 1) * TC], ALU.mult), inc="oy")
                    k.drain("vector")
                    dl_hist.append((dlv, s_end, g_end, oy))
                if DEBUG == "mid" and bl == 0 and cp == 0:
                    k.wait("gpsimd", "oy", dl_hist[-1][3])
                    zdump = sqln
                    k.wait("vector", "oy", dl_hist[-1][3])
                    k.do("vector", lambda e: e.tensor_copy(zdump[:, 0:256], pb[6][:, 0:256]))
                    k.drain("vector")
                    zv = k.enop("vector", "v")
                    k.wait("gpsimd", "v", zv)
                    k.dma("gpsimd", dbg[:, 15296:15552], zdump)
                    k.dma("gpsimd", dbg[:, 15552:15680], spr1)
                    k.dma("gpsimd", dbg[:, 15680:15808], spr2)
                    k.dma("gpsimd", dbg[:, 15808:15936], XNFREE[4])
                    k.dma("gpsimd", dbg[:, 15936:16064], XNFREE[5])
                    k.dma("gpsimd", dbg[:, 0:2048], XN.bitcast(F32)[:, 0:2048])
                    for o_ in range(NOCT):
                        k.dma("gpsimd", dbg[:, 2048 + o_ * TC:2048 + (o_ + 1) * TC], u_all[o_])
                    k.dma("gpsimd", dbg[:, 6144:8192], bcb)
                    k.dma("gpsimd", dbg[:, 8192:10240], ccb)
                    k.dma("gpsimd", dbg[:, 10240:10496], dlt[1])
                    k.dma("gpsimd", dbg[:, 10496:11520], dag)
                    k.dma("gpsimd", dbg[:, 11520:12032], hg)
                    k.dma("gpsimd", dbg[:, 12032:12288], yoct)
                    k.dma("gpsimd", dbg[:, 12288:14336], ybf)
                    k.dma("gpsimd", dbg[0:64, 14336:14464], DPST)
                    k.dma("gpsimd", dbg[0:64, 14464:14592], xdsa[0:64, :])
                    k.dma("gpsimd", dbg[0:64, 14592:14720], xdsb[0:64, :])
                    k.dma("gpsimd", dbg[:, 14720:15040], AUX[:, 64:384])
                    k.dma("gpsimd", dbg[0:64, 15040:15168], wt_dp[:, 15 * 128:16 * 128])
                    k.dma("gpsimd", dbg[:, 15168:15296], XPST)
                    k.wait("gpsimd", "dGP", k.val("dGP"))
                k.wait("tensor", "oy", dl_hist[-1][3])
                op_mm = []
                for oct in range(NOCT):
                    buf = oct % 2
                    if oct >= 2:
                        k.wait("sync", "top", op_mm[oct - 2])
                    chw, vw = k.dma("sync", wopb[buf], wt_op[oct * 128:(oct + 1) * 128, :], ch="dOP%d" % buf)
                    k.wait("tensor", chw, vw)
                    v_ = None
                    for mo in range(8):
                        bank = pb[mo]
                        v_ = k.do("tensor", lambda e, bank=bank, buf=buf, mo=mo, oct=oct, TC=TC: e.matmul(
                            bank[:, 0:TC], wopb[buf][:, mo * 128:(mo + 1) * 128], YBF[:, oct * TC:(oct + 1) * TC],
                            start=(oct == 0), stop=(oct == NOCT - 1)), inc=("top" if mo == 7 else None))
                    op_mm.append(v_)
                k.wait("vector", "top", op_mm[-1])
                for dt in range(NDT):
                    bank = pb[dt]
                    k.do("vector", lambda e, bank=bank, dt=dt, TC=TC: e.tensor_tensor(
                        mcp[dt], XN[:, dt * 512:dt * 512 + TC], bank[:, 0:TC], ALU.add))
                k.drain("vector")
                vv = k.enop("vector", "v")
                k.wait("tensor", "v", vv)
                k.wait("scalar", "v", vv)
                for dt in range(NDT):
                    if dt > 0:
                        k.wait("scalar", "tmm", k.val("tmm"))
                    k.do("scalar", lambda e, dt=dt: e.activation(out=spr1, in_=mcp[dt][:, 0:128], func=AF.Square))
                    sv = k.do("scalar", lambda e, dt=dt: e.activation(out=spr2, in_=mcp[dt][:, 128:256], func=AF.Square), inc="s")
                    k.wait("tensor", "s", sv)
                    k.do("tensor", lambda e, dt=dt: e.matmul(pb[4][0:1, 0:128], ONE[:, 0:1], spr1, start=(dt == 0), stop=(dt == NDT - 1)))
                    tv = k.do("tensor", lambda e, dt=dt: e.matmul(pb[5][0:1, 0:128], ONE[:, 0:1], spr2, start=(dt == 0), stop=(dt == NDT - 1)), inc="tmm")
                k.wait("vector", "tmm", k.val("tmm"))
                k.do("vector", lambda e: e.tensor_scalar_mul(row(R_VAR, 128), pb[4][0:1, 0:128], 1.0 / D))
                k.do("vector", lambda e: e.tensor_scalar_mul(ROWS[0:1, R_VAR + 128:R_VAR + 256], pb[5][0:1, 0:128], 1.0 / D))
                k.drain("vector")
                emit_rsqrt(R_INV, R_VAR, EPS_RMS, TC)
                vv = k.enop("vector", "v")
                k.wait("tensor", "v", vv)
                tv = k.do("tensor", lambda e, TC=TC: e.matmul(pb[4][:, 0:TC], ONE[0:1, 0:128], row(R_INV, TC), start=True, stop=True), inc="tmm")
                k.wait("vector", "tmm", tv)
                for dt in range(NDT):
                    k.do("vector", lambda e, dt=dt, TC=TC: e.tensor_tensor(mcp[dt], mcp[dt], pb[4][:, 0:TC], ALU.mult))
                    k.drain("vector")
                    k.do("vector", lambda e, dt=dt: e.tensor_scalar_mul(mcp[dt], mcp[dt], AUX[:, 392 + dt:393 + dt]))
                    k.drain("vector")
                vv = k.enop("vector", "v")
                k.wait("tensor", "v", vv)
                for tb in range(2):
                    for dt in range(NDT):
                        bank = pb[dt % 2]
                        idx = tb * NDT + dt
                        if idx >= 2:
                            k.wait("tensor", "h1f", k.val("h1f") - 1)
                        tv = k.do("tensor", lambda e, bank=bank, dt=dt, tb=tb: e.transpose(
                            bank[:, 0:128], mcp[dt][:, tb * 128:(tb + 1) * 128], IDNF[:, :]), inc="tmm")
                        k.wait("vector", "tmm", tv)
                        if k.val("dGP") > 0:
                            k.wait("vector", "dGP", k.val("dGP"))
                        k.do("vector", lambda e, bank=bank, dt=dt: e.tensor_copy(H1ST[:, dt * 128:(dt + 1) * 128], bank[:, 0:128]), inc="h1f")
                    k.drain("vector")
                    vv = k.enop("vector", "v")
                    k.wait("gpsimd", "v", vv)
                    k.dma("gpsimd", h1s[tok0 + tb * 128: tok0 + (tb + 1) * 128, :], H1ST)
            k.barrier()

        def emit_blffn_half(half):
            ar = Arena(X, 1 - half)
            sq = ar.alloc1(512)
            usc = ar.alloc1(512)
            acc = ar.alloc(4, 1024)
            hall_u = ar.alloc(8, 1024)
            hall = []
            for u in hall_u:
                ub = u.bitcast(BF16)
                hall += [ub[:, i * 512:(i + 1) * 512] for i in range(4)]
            m2 = [hall_u[dt][:, 0:512] for dt in range(NDT)]
            hmx = ar.alloc1(512)
            hmn = ar.alloc1(512)
            smb = ar.alloc1(512)
            sgb = ar.alloc1(512)
            sub = ar.alloc1(512)
            qsb = ar.alloc1(512)
            nfc = BLF // 128
            wgb = [WBUF[:, 0:1024], WBUF[:, 1024:2048]]
            wub = [WBUF[:, 2048:3072], WBUF[:, 3072:4096]]
            wdb = [WBUF[:, 4096:5120], WBUF[:, 5120:6144]]
            H1LD = WSTGF[:, 3072:4096]
            actset(18)

            def h1_sweep(consume, tok0):
                for tg in range(4):
                    if k.val("hsw") > 0:
                        k.wait("sync", "hsw", k.val("hsw"))
                    chh, vh = k.dma("sync", H1LD, h1s[tok0 + tg * 128: tok0 + (tg + 1) * 128, :], ch="dH%d" % (tg % 2))
                    k.wait("tensor", chh, vh)
                    for dt in range(NDT):
                        bank = pb[dt % 2]
                        idx = tg * NDT + dt
                        if idx >= 2:
                            k.wait("tensor", "hseva", k.val("hseva") - 1)
                        tv = k.do("tensor", lambda e, bank=bank, dt=dt: e.transpose(
                            bank[:, 0:128], H1LD[:, dt * 128:(dt + 1) * 128], IDNF[:, :]),
                            inc=("hsw" if dt == NDT - 1 else "tmm"))
                        k.wait("vector", ("hsw" if dt == NDT - 1 else "tmm"), tv)
                        consume(dt, tg, bank)
                k.drain("vector")
                vv = k.enop("vector", "v")
                k.wait("tensor", "v", vv)
                k.wait("scalar", "v", vv)
                return vv

            for tt in range(4):
                tok0 = half * HTOK + tt * 512
                k.do("vector", lambda e: e.memset(hmx, 0.0))
                k.do("vector", lambda e: e.memset(hmn, 0.0))
                k.drain("vector")
                zz = k.enop("vector", "v")
                k.wait("tensor", "v", zz)

                def c_xm(dt, tg, bank):
                    k.do("vector", lambda e, bank=bank, tg=tg: e.tensor_tensor(
                        hmx[:, tg * 128:(tg + 1) * 128], hmx[:, tg * 128:(tg + 1) * 128], bank[:, 0:128], ALU.max))
                    k.drain("vector")
                    k.do("vector", lambda e, bank=bank, tg=tg: e.tensor_tensor(
                        hmn[:, tg * 128:(tg + 1) * 128], hmn[:, tg * 128:(tg + 1) * 128], bank[:, 0:128], ALU.min))
                    k.drain("vector")
                    k.enop("vector", "hseva")
                h1_sweep(c_xm, tok0)
                k.do("vector", lambda e: e.tensor_scalar_mul(hmn, hmn, -1.0))
                k.drain("vector")
                k.do("vector", lambda e: e.tensor_tensor(hmx, hmx, hmn, ALU.max))
                k.drain("vector")
                vv = k.enop("vector", "v")
                k.wait("tensor", "v", vv)
                emit_stat_bcast_fm(hmx, smb, 1040)
                k.do("vector", lambda e: e.tensor_scalar(smb, smb, 1e-5, None, ALU.max))
                k.drain("vector")
                k.do("vector", lambda e: e.reciprocal(qsb, smb))
                k.drain("vector")
                k.do("vector", lambda e: e.tensor_scalar_mul(qsb, qsb, 127.0))
                k.do("vector", lambda e: e.tensor_scalar_mul(smb, smb, 1.0 / 127.0))
                k.drain("vector")
                k.do("vector", lambda e: e.tensor_scalar_mul(sgb, smb, AUX[:, 424:425]))
                k.do("vector", lambda e: e.tensor_scalar_mul(sub, smb, AUX[:, 425:426]))
                k.drain("vector")
                vv = k.enop("vector", "v")
                k.wait("tensor", "v", vv)

                def c_xq(dt, tg, bank):
                    k.do("vector", lambda e, bank=bank, tg=tg: e.tensor_tensor(
                        sq[:, tg * 128:(tg + 1) * 128], bank[:, 0:128], qsb[:, tg * 128:(tg + 1) * 128], ALU.mult))
                    k.drain("vector")
                    k.do("vector", lambda e, bank=bank, dt=dt, tg=tg: e.tensor_scalar(
                        XN[:, dt * 512 + tg * 128:dt * 512 + (tg + 1) * 128], sq[:, tg * 128:(tg + 1) * 128],
                        MAGIC, -MAGIC, ALU.add, ALU.add))
                    k.drain("vector")
                    k.enop("vector", "hseva")
                h1_sweep(c_xq, tok0)
                k.do("vector", lambda e: e.memset(hmx, 0.0))
                k.do("vector", lambda e: e.memset(hmn, 0.0))
                k.drain("vector")
                vv = k.enop("vector", "v")
                k.wait("tensor", "v", vv)
                k.wait("scalar", "v", vv)
                gu_hist = []
                for fc in range(nfc):
                    buf = fc % 2
                    gsrc = wt_bg.rearrange("(kc p) o -> p kc o", p=128)[:, :, fc * 128:(fc + 1) * 128]
                    usrc = wt_bu.rearrange("(kc p) o -> p kc o", p=128)[:, :, fc * 128:(fc + 1) * 128]
                    if fc >= 2:
                        k.wait("sync", "tgu", gu_hist[fc - 2][0])
                    chg, vg = k.dma("sync", wgb[buf].rearrange("p (kc o) -> p kc o", kc=8), gsrc, ch="dWG%d" % buf)
                    chu, vu = k.dma("sync", wub[buf].rearrange("p (kc o) -> p kc o", kc=8), usrc, ch="dWU%d" % buf)
                    k.wait("tensor", chg, vg)
                    k.wait("tensor", chu, vu)
                    if fc >= 2:
                        k.wait("tensor", "hev", gu_hist[fc - 2][1])
                    bg, bu = pb[0 + 2 * buf], pb[1 + 2 * buf]
                    tgu_v = None
                    for kc in range(8):
                        k.do("tensor", lambda e, b=bg, w_=wgb[buf], kc=kc: e.matmul(
                            b[:, 0:512], w_[:, kc * 128:(kc + 1) * 128], XN[:, kc * 512:(kc + 1) * 512],
                            start=(kc == 0), stop=(kc == 7)))
                    for kc in range(8):
                        tgu_v = k.do("tensor", lambda e, b=bu, w_=wub[buf], kc=kc: e.matmul(
                            b[:, 0:512], w_[:, kc * 128:(kc + 1) * 128], XN[:, kc * 512:(kc + 1) * 512],
                            start=(kc == 0), stop=(kc == 7)), inc=("tgu" if kc == 7 else None))
                    k.wait("vector", "tgu", tgu_v)
                    if fc >= 1:
                        k.wait("vector", "hev", gu_hist[fc - 1][1])
                    k.do("vector", lambda e, b=bg: e.tensor_tensor(sq, b[:, 0:512], sgb, ALU.mult))
                    k.drain("vector")
                    sgv = k.enop("vector", "v")
                    k.wait("scalar", "v", sgv)
                    sv = k.do("scalar", lambda e: e.activation(out=sq, in_=sq, func=AF.Tanh, scale=0.5), inc="s")
                    k.wait("vector", "s", sv)
                    k.do("vector", lambda e: e.tensor_scalar(sq, sq, 0.5, 0.5, ALU.mult, ALU.add))
                    k.drain("vector")
                    k.do("vector", lambda e, b=bu: e.tensor_tensor(usc, b[:, 0:512], sub, ALU.mult))
                    k.drain("vector")
                    hev_v = k.do("vector", lambda e, h=hall[fc]: e.tensor_tensor(h, sq, usc, ALU.mult), inc="hev")
                    k.drain("vector")
                    k.do("vector", lambda e, h=hall[fc]: e.tensor_tensor(hmx, hmx, h, ALU.max))
                    k.drain("vector")
                    k.do("vector", lambda e, h=hall[fc]: e.tensor_tensor(hmn, hmn, h, ALU.min))
                    k.drain("vector")
                    gu_hist.append((tgu_v, hev_v))
                vv = k.enop("vector", "v")
                k.wait("tensor", "v", vv)
                k.do("vector", lambda e: e.tensor_scalar_mul(hmn, hmn, -1.0))
                k.drain("vector")
                k.do("vector", lambda e: e.tensor_tensor(hmx, hmx, hmn, ALU.max))
                k.drain("vector")
                vv = k.enop("vector", "v")
                k.wait("tensor", "v", vv)
                emit_stat_bcast_fm(hmx, smb, 1044)
                k.do("vector", lambda e: e.tensor_scalar(smb, smb, 1e-5, None, ALU.max))
                k.do("vector", lambda e: e.tensor_scalar(AUX[:, 1044:1048], AUX[:, 1044:1048], 1e-5, None, ALU.max))
                k.drain("vector")
                k.do("vector", lambda e: e.reciprocal(qsb, smb))
                k.drain("vector")
                k.do("vector", lambda e: e.tensor_scalar_mul(qsb, qsb, 127.0))
                k.do("vector", lambda e: e.tensor_scalar_mul(AUX[:, 1048:1052], AUX[:, 1044:1048], AUX[:, 426:427]))
                k.drain("vector")
                k.do("vector", lambda e: e.tensor_scalar_mul(AUX[:, 1048:1052], AUX[:, 1048:1052], 1.0 / 127.0))
                k.drain("vector")
                for fc in range(nfc):
                    k.do("vector", lambda e, h=hall[fc]: e.tensor_tensor(sq, h, qsb, ALU.mult))
                    k.drain("vector")
                    k.do("vector", lambda e, h=hall[fc]: e.tensor_scalar(h, sq, MAGIC, -MAGIC, ALU.add, ALU.add))
                    k.drain("vector")
                vv = k.enop("vector", "v")
                k.wait("tensor", "v", vv)
                w2_hist = []
                for th in range(2):
                    for fc in range(nfc):
                        it2 = th * nfc + fc
                        buf = it2 % 2
                        dsrc = wt_bd[fc * 128:(fc + 1) * 128, :]
                        if it2 >= 2:
                            k.wait("sync", "tw2", w2_hist[it2 - 2])
                        chd, vd = k.dma("sync", wdb[buf], dsrc, ch="dWD%d" % buf)
                        k.wait("tensor", chd, vd)
                        tw2_v = None
                        for tb in range(2):
                            for nh in range(2):
                                bank = pb[4 + 2 * tb + nh]
                                tw2_v = k.do("tensor", lambda e, bank=bank, h=hall[fc], th=th, tb=tb, nh=nh, buf=buf, fc=fc, nfc=nfc: e.matmul(
                                    bank[:, 0:512], h[:, th * 256 + tb * 128: th * 256 + (tb + 1) * 128],
                                    wdb[buf][:, nh * 512:(nh + 1) * 512],
                                    start=(fc == 0), stop=(fc == nfc - 1)),
                                    inc=("tw2" if (tb == 1 and nh == 1) else None))
                        w2_hist.append(tw2_v)
                    k.wait("vector", "tw2", k.val("tw2"))
                    for tb in range(2):
                        tbg = th * 2 + tb
                        for nh in range(2):
                            bank = pb[4 + 2 * tb + nh]
                            a_ = acc[tbg][:, nh * 512:(nh + 1) * 512]
                            k.do("vector", lambda e, b=bank, a_=a_, tbg=tbg: e.tensor_scalar_mul(a_, b[:, 0:512], AUX[:, 1048 + tbg:1049 + tbg]))
                    k.drain("vector")
                    pv = k.enop("vector", "v")
                    k.wait("tensor", "v", pv)

                def c_m2(dt, tg, bank):
                    fb = pb[4 + (dt % 2)]
                    k.do("vector", lambda e, bank=bank: e.tensor_copy(sq[:, 0:128], bank[:, 0:128]))
                    k.drain("vector")
                    tv2 = k.do("tensor", lambda e, fb=fb, a_=acc[tg], dt=dt: e.transpose(
                        fb[:, 0:128], a_[:, dt * 128:(dt + 1) * 128], IDNF[:, :]), inc="tmm")
                    k.wait("vector", "tmm", tv2)
                    k.do("vector", lambda e, fb=fb, dt=dt, tg=tg: e.tensor_tensor(
                        m2[dt][:, tg * 128:(tg + 1) * 128], sq[:, 0:128], fb[:, 0:128], ALU.add))
                    k.drain("vector")
                    k.enop("vector", "hseva")
                h1_sweep(c_m2, tok0)
                for dt in range(NDT):
                    if dt > 0:
                        k.wait("scalar", "tmm", k.val("tmm"))
                    sv = k.do("scalar", lambda e, dt=dt: e.activation(out=sq, in_=m2[dt], func=AF.Square), inc="s")
                    k.wait("tensor", "s", sv)
                    tv = k.do("tensor", lambda e, dt=dt: e.matmul(pb[6][0:1, 0:512], ONE[:, 0:1], sq,
                                                                 start=(dt == 0), stop=(dt == NDT - 1)), inc="tmm")
                k.wait("vector", "tmm", k.val("tmm"))
                k.do("vector", lambda e: e.tensor_scalar_mul(row(R_VAR, 512), pb[6][0:1, 0:512], 1.0 / D))
                k.drain("vector")
                emit_rsqrt(R_INV, R_VAR, EPS_RMS, 512)
                vv = k.enop("vector", "v")
                k.wait("tensor", "v", vv)
                tv = k.do("tensor", lambda e: e.matmul(pb[6][:, 0:512], ONE[0:1, 0:128], row(R_INV, 512), start=True, stop=True), inc="tmm")
                k.wait("vector", "tmm", tv)
                for dt in range(NDT):
                    xs = X[:, dt * TOK + tok0: dt * TOK + tok0 + 512]
                    k.do("vector", lambda e, dt=dt: e.tensor_tensor(sq, m2[dt], pb[6][:, 0:512], ALU.mult))
                    k.drain("vector")
                    k.do("vector", lambda e, dt=dt: e.tensor_scalar_mul(sq, sq, AUX[:, 400 + dt:401 + dt]))
                    k.drain("vector")
                    k.do("vector", lambda e, xs=xs: e.tensor_tensor(xs, xs, sq, ALU.add))
                    k.drain("vector")
                vv = k.enop("vector", "v")
                k.wait("tensor", "v", vv)
                k.wait("scalar", "v", vv)
            k.barrier()

        if stage_on("a1"):
            emit_ffn_both_halves("l0f")

        if stage_on("a2"):
            spill(1)
            emit_mamba_half(0)
            spill(0)
            restore(1)
            emit_mamba_half(1)
            if DEBUG != "h1" and DEBUG != "mid":
                emit_blffn_half(1)
                spill(1)
                restore(0)
                emit_blffn_half(0)
                restore(1)
            elif DEBUG == "mid":
                pass
            else:
                pass

        if stage_on("a3") and DEBUG == "":
            emit_ffn_both_halves("l0f")

        if PHASE_B:
            pass
        elif DEBUG == "h1":
            k.dma("gpsimd", y.rearrange("b l dd -> (b l) dd"), h1s[:, :])
            k.wait("gpsimd", "dGP", k.val("dGP"))
        else:
            yflat = y.rearrange("b l dd -> (b l) dd")
            tstg2 = WSTGF[:, 0:1024]
            for tg in range(TOK // 128):
                for dt in range(NDT):
                    bank = pb[dt % 2]
                    idx = tg * NDT + dt
                    if idx >= 2:
                        k.wait("tensor", "stx", idx - 1)
                    tv = k.do("tensor", lambda e, b=bank, dt=dt, tg=tg: e.transpose(
                        b[:, 0:128], X[:, dt * TOK + tg * 128: dt * TOK + (tg + 1) * 128], IDNF[:, :]), inc="tmm")
                    k.wait("vector", "tmm", tv)
                    if tg > 0 and dt == 0 and k.val("dGP") > 0:
                        k.wait("vector", "dGP", k.val("dGP"))
                    k.do("vector", lambda e, b=bank, dt=dt: e.tensor_copy(tstg2[:, dt * 128:(dt + 1) * 128], b[:, 0:128]))
                    k.enop("vector", "stx")
                k.drain("vector")
                vv = k.enop("vector", "v")
                k.wait("gpsimd", "v", vv)
                k.dma("gpsimd", yflat[tg * 128:(tg + 1) * 128, :], tstg2)
            k.wait("gpsimd", "dGP", k.val("dGP"))

        with nc.Block() as block:
            @block.sync
            def _(e):
                for f in k.ops["sync"]:
                    f(e)

            @block.gpsimd
            def _(e):
                for f in k.ops["gpsimd"]:
                    f(e)

            @block.scalar
            def _(e):
                for f in k.ops["scalar"]:
                    f(e)

            @block.vector
            def _(e):
                for f in k.ops["vector"]:
                    f(e)

            @block.tensor
            def _(e):
                for f in k.ops["tensor"]:
                    f(e)

    return nc


KEY_MAP = {
    "ipw": "L0_attn_in_proj_w", "cw": "L0_attn_conv_w", "cb": "L0_attn_conv_b",
    "xpw": "L0_attn_x_proj_w", "dpw": "L0_attn_dt_proj_w", "dpb": "L0_attn_dt_proj_b",
    "alog": "L0_attn_A_log", "dvec": "L0_attn_D", "opw": "L0_attn_out_proj_w",
    "n1w": "L0_attn_norm1_w", "blg": "L0_attn_gate_w", "blu": "L0_attn_up_w",
    "bld": "L0_attn_down_w", "n2w": "L0_attn_norm2_w",
    "fg": "L0_ffn_w_gate", "fu": "L0_ffn_w_up", "fd": "L0_ffn_w_down",
    "wq": "L1_attn_w_q", "wk": "L1_attn_w_k", "wv": "L1_attn_w_v", "wo": "L1_attn_w_o",
    "gw": "L1_ffn_gate_w", "w1": "L1_ffn_w1", "w2": "L1_ffn_w2", "w3": "L1_ffn_w3",
}


def kernel(**inputs):
    nc = build()
    x = np.ascontiguousarray(inputs["L0_attn_x"])
    in_maps = []
    for c in range(8):
        m = {"x": np.ascontiguousarray(x[2 * c:2 * c + 2])}
        for kk, src in KEY_MAP.items():
            m[kk] = np.ascontiguousarray(inputs[src])
        in_maps.append(m)
    res = run_bass_kernel_spmd(nc, in_maps, core_ids=list(range(8)))
    if DEBUG == "mid":
        np.save("/tmp/dbg0.npy", res.results[0]["dbg"])
    if PHASE_B:
        return np.concatenate([res.results[c]["y"] for c in range(8)], axis=1)
    return np.concatenate([res.results[c]["y"] for c in range(8)], axis=0)
